# version 16
# speedup vs baseline: 1.5678x; 1.0709x over previous
import numpy as np
import ml_dtypes
from contextlib import ExitStack

import concourse.bass as bass
import concourse.mybir as mybir
from concourse.bass_utils import run_bass_kernel_spmd

F32 = mybir.dt.float32
BF16 = mybir.dt.bfloat16
I32 = mybir.dt.int32
AF = mybir.ActivationFunctionType
ALU = mybir.AluOpType
AX = mybir.AxisListType

LN_EPS = 1e-5
RMS_EPS = 1e-5
NORM_CLAMP = 1e-4
LIMIT = 7.0
ALPHA = 1.702
NEG_BIG = -1.0e30
NEG_FILL = -3.0e38


def build_program(S, Tq, H=1024, E=32, F=1024, cap=1024):
    HC = H // 128
    NK = S // 128
    NQT = Tq // 128
    QBS = 256
    QTPB = QBS // 128
    NQB = Tq // QBS
    NVB = 4
    VPL = 4
    NVL = NK // VPL
    FC = F // 128
    WGR = 4

    nc = bass.Bass("TRN2", target_bir_lowering=False, debug=False)

    xb = nc.dram_tensor("xb", [S, H], F32, kind="ExternalInput").ap()
    w1gluT_d = nc.dram_tensor("w1gluT", [E, H, F], F32, kind="ExternalInput").ap()
    w1linT_d = nc.dram_tensor("w1linT", [E, H, F], F32, kind="ExternalInput").ap()
    w2T_d = nc.dram_tensor("w2T", [E, F, H], F32, kind="ExternalInput").ap()
    gate_wT = nc.dram_tensor("gate_wT", [H, E], F32, kind="ExternalInput").ap()
    gate_b_row = nc.dram_tensor("gate_b_row", [1, E], F32, kind="ExternalInput").ap()
    ns_cols_d = nc.dram_tensor("ns_cols", [128, H // 128], F32, kind="ExternalInput").ap()
    b1glu_cols = nc.dram_tensor("b1glu_cols", [E, 128, FC], F32, kind="ExternalInput").ap()
    b1lin_cols = nc.dram_tensor("b1lin_cols", [E, 128, FC], F32, kind="ExternalInput").ap()
    b2_in = nc.dram_tensor("b2_in", [E, H], F32, kind="ExternalInput").ap()
    eye_bf_d = nc.dram_tensor("eye_bf", [128, 128], BF16, kind="ExternalInput").ap()
    eye_f32_d = nc.dram_tensor("eye_f32", [128, 128], F32, kind="ExternalInput").ap()
    lstrict_d = nc.dram_tensor("lstrict", [128, 128], F32, kind="ExternalInput").ap()
    ecap_row_d = nc.dram_tensor("ecap_row", [1, E], F32, kind="ExternalInput").ap()
    iota_t = nc.dram_tensor("iota_t", [128, Tq // 128], I32, kind="ExternalInput").ap()

    out_d = nc.dram_tensor("out", [Tq, H], F32, kind="ExternalOutput").ap()
    out_t = out_d.rearrange("(t p) h -> t p h", p=128)

    scr_v = nc.dram_tensor("scr_v", [NK, 128, H], BF16, kind="Internal").ap()

    scr_xn = nc.dram_tensor("scr_xn", [NQT, 128, H], F32, kind="Internal").ap()
    scr_pt = nc.dram_tensor("scr_pt", [NQT, 128, H], F32, kind="Internal").ap()
    scr_tb = nc.dram_tensor("scr_tb", [NQT, 128, H], BF16, kind="Internal").ap()
    scr_idx = nc.dram_tensor("scr_idx", [E * cap], I32, kind="Internal").ap()
    scr_os = nc.dram_tensor("scr_os", [E * cap, H], F32, kind="Internal").ap()
    scr_bas = nc.dram_tensor("scr_bas", [NQT * E], F32, kind="Internal").ap()
    scr_w1g = nc.dram_tensor("scr_w1g", [E, H, F], BF16, kind="Internal").ap()
    scr_w1l = nc.dram_tensor("scr_w1l", [E, H, F], BF16, kind="Internal").ap()

    xbt = xb.rearrange("(t p) h -> t p h", p=128)

    with ExitStack() as top:
        def sb(name, shape, dtype):
            return top.enter_context(nc.sbuf_tensor(name, shape, dtype))

        def sem(name):
            return top.enter_context(nc.semaphore(name))

        gb_bcast = sb("gb_bcast", [128, E], F32)
        b2s = sb("b2s", [E, H], F32)
        gwT = sb("gwT", [128, HC * E], F32)
        b1g_sb = sb("b1g_sb", [128, E * FC], F32)
        b1l_sb = sb("b1l_sb", [128, E * FC], F32)
        eyef_sb = sb("eyef_sb", [128, 128], F32)
        ns_col = sb("ns_col", [128, HC], F32)
        cbias = sb("cbias", [128, 2], F32)
        lst_sb = sb("lst_sb", [128, 128], F32)
        ecap_bc = sb("ecap_bc", [128, E], F32)
        ecapr_sb = sb("ecapr_sb", [1, E], F32)
        onesf_col = sb("onesf_col", [128, 1], F32)
        ones1 = sb("ones1", [1, 128], F32)
        eyeb_sb = sb("eyeb_sb", [128, 128], BF16)

        bar = sem("bar")
        d_cst = sem("d_cst")
        d_cst2 = sem("d_cst2")
        d_cst3 = sem("d_cst3")
        c_nsA = sem("c_nsA")
        c_nsB = sem("c_nsB")
        s_xa = [sem(f"s_xa{i}") for i in range(4)]
        s_s1 = sem("s_s1")
        s_sq = sem("s_sq")
        s_dv1 = sem("s_dv1")
        s_sc2 = sem("s_sc2")
        s_dv2 = sem("s_dv2")
        s_xnu = sem("s_xnu")
        s_tp = sem("s_tp")
        s_ut = sem("s_ut")
        s_vw = sem("s_vw")
        s_xnw = sem("s_xnw")
        s_vfree = sem("s_vfree")
        s_dots = sem("s_dots")
        s_exp = sem("s_exp")
        s_pv = sem("s_pv")
        s_vld = [sem(f"s_vld{i}") for i in range(NVB)]
        s_xq = sem("s_xq")
        s_evq = sem("s_evq")
        s_pst = sem("s_pst")
        c_gx = [sem("c_gx0"), sem("c_gx1")]
        cA = sem("cA")
        cB = sem("cB")
        cC = sem("cC")
        cD = sem("cD")
        cTp = sem("cTp")
        cEv = sem("cEv")
        cGm = sem("cGm")
        cMk = sem("cMk")
        c1e = sem("c1e")
        c1c = sem("c1c")
        c2t = sem("c2t")
        c2c = sem("c2c")
        c2m = sem("c2m")
        c2e = sem("c2e")
        s_wda = [sem("s_wda0"), sem("s_wda1")]
        s_wdb = [sem("s_wdb0"), sem("s_wdb1")]
        s_tbw = sem("s_tbw")
        c_zs = sem("c_zs")
        s_zi = sem("s_zi")
        s_iot = sem("s_iot")
        s_sct = sem("s_sct")
        s_ild = sem("s_ild")
        s_dsp = sem("s_dsp")
        s_tgg = [sem("s_tgg0"), sem("s_tgg1")]
        cTg = sem("cTg")
        s_tge = sem("s_tge")
        rkA = sem("rkA")
        rkB = sem("rkB")
        rkC = sem("rkC")
        rkD = sem("rkD")
        s_g4 = [sem(f"s_g4{i}") for i in range(4)]
        s_g4e = sem("s_g4e")
        s_osw = [sem("s_osw0"), sem("s_osw1")]
        s_m1 = sem("s_m1")
        s_si = sem("s_si")
        s_sw = sem("s_sw")
        s_m2 = sem("s_m2")
        s_cpA = sem("s_cpA")
        s_fl = [sem("s_fl0"), sem("s_fl1")]
        s_fs = sem("s_fs")
        s_out = [sem("s_out0"), sem("s_out1")]
        s_wc0 = sem("s_wc0")
        s_trw = sem("s_trw")
        s_wcast = sem("s_wcast")

        with ExitStack() as abctx:
            uT = abctx.enter_context(nc.sbuf_tensor("uT", [128, HC * S], BF16))
            onesc_bf = abctx.enter_context(nc.sbuf_tensor("onesc_bf", [128, 1], BF16))

            with ExitStack() as actx:
                def asb(name, shape, dtype):
                    return actx.enter_context(nc.sbuf_tensor(name, shape, dtype))

                gbr_sb = asb("gbr_sb", [1, E], F32)
                NPA = 4
                xa = [asb(f"xa{i}", [128, H], F32) for i in range(NPA)]
                xn_sb = [asb(f"xnA{i}", [128, H], F32) for i in range(NPA)]
                u_bf = [asb(f"uA{i}", [128, H], BF16) for i in range(NPA)]
                sqscr = asb("sqscr", [128, H], F32)
                stA = asb("stA", [128, NK], F32)
                stB = asb("stB", [128, NK], F32)
                stC = asb("stC", [128, 13 * NK], F32)
                tps = [actx.enter_context(nc.psum_tensor(f"tpsA{i}", [128, H], BF16))
                       for i in range(2)]
                cps = actx.enter_context(nc.psum_tensor("cpsA", [128, 512], F32))

                with nc.Block("phA") as blockA:

                    @blockA.sync
                    def _(sync):
                        sync.dma_start(out=ns_col[:], in_=ns_cols_d).then_inc(d_cst, 16)
                        sync.dma_start(out=gbr_sb[:], in_=gate_b_row).then_inc(d_cst, 16)
                        sync.dma_start(out=b2s[:], in_=b2_in).then_inc(d_cst, 16)
                        sync.dma_start(out=eyeb_sb[:], in_=eye_bf_d).then_inc(d_cst2, 16)
                        sync.dma_start(out=eyef_sb[:], in_=eye_f32_d).then_inc(d_cst2, 16)
                        sync.dma_start(
                            out=gwT[:].rearrange("p (c e) -> p c e", c=HC),
                            in_=gate_wT.rearrange("(c p) e -> p c e", p=128)
                        ).then_inc(d_cst2, 16)
                        sync.dma_start(
                            out=b1g_sb[:].rearrange("p (e c) -> p e c", e=E),
                            in_=b1glu_cols.rearrange("e p c -> p e c")
                        ).then_inc(d_cst2, 16)
                        sync.dma_start(
                            out=b1l_sb[:].rearrange("p (e c) -> p e c", e=E),
                            in_=b1lin_cols.rearrange("e p c -> p e c")
                        ).then_inc(d_cst2, 16)
                        sync.dma_start(out=lst_sb[:], in_=lstrict_d).then_inc(d_cst3, 16)
                        sync.dma_start(out=ecapr_sb[:], in_=ecap_row_d).then_inc(d_cst3, 16)
                        for kt in range(NK):
                            if kt >= 4:
                                sync.wait_ge(s_s1, kt - 3)
                                sync.wait_ge(s_sq, kt - 3)
                            sync.dma_start(out=xa[kt % 4][:], in_=xbt[kt]).then_inc(s_xa[kt % 4], 16)
                        sync.wait_ge(s_sq, NK)
                        sync.wait_ge(s_s1, NK)
                        for kt in range(NK):
                            if kt >= 4:
                                sync.wait_ge(s_xnu, kt - 3)
                            sync.dma_start(out=xa[kt % 4][:], in_=xbt[kt]).then_inc(s_xa[kt % 4], 16)
                        sync.wait_ge(s_xnu, NK)
                        sync.wait_ge(d_cst, 16 * 3)
                        sync.wait_ge(d_cst2, 16 * 5)
                        sync.wait_ge(d_cst3, 16 * 2)
                        sync.nop().then_inc(bar, 1)

                    @blockA.vector
                    def _(vector):
                        vector.memset(ones1[:], 1.0)
                        vector.memset(onesc_bf[:], 1.0)
                        vector.memset(onesf_col[:], 1.0)
                        vector.memset(cbias[:, 0:1], ALPHA * LIMIT)
                        vector.memset(cbias[:, 1:2], 2.0 * LIMIT)
                        vector.wait_ge(d_cst, 16 * 3)
                        vector.tensor_scalar_mul(b2s[:], b2s[:], ALPHA)
                        vector.drain().then_inc(c_nsB, 1)
                        vector.wait_ge(d_cst2, 16 * 5)
                        vector.tensor_scalar(out=b1g_sb[:], in0=b1g_sb[:],
                                             scalar1=-1.0, scalar2=LIMIT,
                                             op0=ALU.mult, op1=ALU.add)
                        vector.tensor_scalar(out=b1l_sb[:], in0=b1l_sb[:],
                                             scalar1=-1.0, scalar2=LIMIT,
                                             op0=ALU.mult, op1=ALU.add)
                        vector.drain()
                        vector.wait_ge(c_nsA, 1)
                        vector.tensor_copy(out=gb_bcast[:], in_=cps[:, 0:E])
                        vector.drain().then_inc(c_nsB, 1)
                        vector.wait_ge(c_nsA, 2)
                        vector.tensor_copy(out=ecap_bc[:], in_=cps[:, 0:E])
                        vector.drain().then_inc(c_nsB, 1)

                        for kt in range(NK):
                            vector.wait_ge(s_xa[kt % 4], 16 * (kt // 4 + 1))
                            vector.reduce_sum(stA[:, kt:kt + 1], xa[kt % 4][:], axis=AX.X)
                            vector.drain().then_inc(s_s1, 1)
                        vector.wait_ge(s_sq, NK)
                        st = stC
                        vector.tensor_scalar_mul(st[:, 0 * NK:1 * NK], stA[:], 1.0 / H)
                        vector.drain()
                        vector.tensor_mul(st[:, 1 * NK:2 * NK], st[:, 0 * NK:1 * NK],
                                          st[:, 0 * NK:1 * NK])
                        vector.drain()
                        vector.scalar_tensor_tensor(out=st[:, 2 * NK:3 * NK], in0=stB[:],
                                                    scalar=1.0 / H, in1=st[:, 1 * NK:2 * NK],
                                                    op0=ALU.mult, op1=ALU.subtract)
                        vector.drain()
                        vector.tensor_scalar_add(st[:, 3 * NK:4 * NK],
                                                 st[:, 2 * NK:3 * NK], LN_EPS)
                        vector.drain()
                        vector.reciprocal(st[:, 4 * NK:5 * NK], st[:, 3 * NK:4 * NK])
                        vector.drain()
                        vector.tensor_mul(st[:, 5 * NK:6 * NK], st[:, 2 * NK:3 * NK],
                                          st[:, 4 * NK:5 * NK])
                        vector.drain().then_inc(s_dv1, 1)
                        vector.wait_ge(s_sc2, 1)
                        vector.tensor_scalar_max(st[:, 8 * NK:9 * NK],
                                                 st[:, 7 * NK:8 * NK], NORM_CLAMP)
                        vector.drain()
                        vector.reciprocal(st[:, 9 * NK:10 * NK], st[:, 8 * NK:9 * NK])
                        vector.drain()
                        vector.tensor_mul(st[:, 10 * NK:11 * NK], st[:, 6 * NK:7 * NK],
                                          st[:, 9 * NK:10 * NK])
                        vector.drain()
                        vector.scalar_tensor_tensor(out=st[:, 11 * NK:12 * NK],
                                                    in0=st[:, 0 * NK:1 * NK], scalar=-1.0,
                                                    in1=st[:, 6 * NK:7 * NK],
                                                    op0=ALU.mult, op1=ALU.mult)
                        vector.scalar_tensor_tensor(out=st[:, 12 * NK:13 * NK],
                                                    in0=st[:, 0 * NK:1 * NK], scalar=-1.0,
                                                    in1=st[:, 10 * NK:11 * NK],
                                                    op0=ALU.mult, op1=ALU.mult)
                        vector.drain().then_inc(s_dv2, 1)
                        for kt in range(NK):
                            p = kt % 4
                            vector.wait_ge(s_xa[kt % 4], 16 * (NK // 4 + kt // 4 + 1))
                            if kt >= 4:
                                vector.wait_ge(s_vfree, kt - 3)
                                vector.wait_ge(s_tp, kt - 3)
                            vector.tensor_scalar(out=xn_sb[p][:], in0=xa[p][:],
                                                 scalar1=st[:, 6 * NK + kt:6 * NK + kt + 1],
                                                 scalar2=st[:, 11 * NK + kt:11 * NK + kt + 1],
                                                 op0=ALU.mult, op1=ALU.add)
                            vector.tensor_scalar(out=u_bf[p][:], in0=xa[p][:],
                                                 scalar1=st[:, 10 * NK + kt:10 * NK + kt + 1],
                                                 scalar2=st[:, 12 * NK + kt:12 * NK + kt + 1],
                                                 op0=ALU.mult, op1=ALU.add)
                            vector.drain().then_inc(s_xnu, 1)
                            if kt >= 1:
                                kp = kt - 1
                                vector.wait_ge(s_tp, kp + 1)
                                dst = uT[:].rearrange("p (c s) -> p c s", c=HC)[
                                    :, :, kp * 128:(kp + 1) * 128]
                                srcv = tps[kp % 2][:].rearrange("p (c f) -> p c f", f=128)
                                vector.tensor_copy(out=dst, in_=srcv)
                                vector.drain().then_inc(s_ut, 1)
                        kp = NK - 1
                        vector.wait_ge(s_tp, kp + 1)
                        dst = uT[:].rearrange("p (c s) -> p c s", c=HC)[
                            :, :, kp * 128:(kp + 1) * 128]
                        srcv = tps[kp % 2][:].rearrange("p (c f) -> p c f", f=128)
                        vector.tensor_copy(out=dst, in_=srcv)
                        vector.drain().then_inc(s_ut, 1)
                        vector.drain().then_inc(bar, 1)

                    @blockA.scalar
                    def _(scalar):
                        for kt in range(NK):
                            scalar.wait_ge(s_xa[kt % 4], 16 * (kt // 4 + 1))
                            scalar.activation(out=sqscr[:], in_=xa[kt % 4][:],
                                              func=AF.Square, accum_out=stB[:, kt:kt + 1])
                            scalar.drain().then_inc(s_sq, 1)
                        scalar.wait_ge(s_dv1, 1)
                        scalar.activation(out=stC[:, 6 * NK:7 * NK],
                                          in_=stC[:, 4 * NK:5 * NK], func=AF.Sqrt)
                        scalar.activation(out=stC[:, 7 * NK:8 * NK],
                                          in_=stC[:, 5 * NK:6 * NK], func=AF.Sqrt,
                                          scale=float(H))
                        scalar.drain().then_inc(s_sc2, 1)
                        scalar.drain().then_inc(bar, 1)

                    @blockA.tensor
                    def _(tensor):
                        tensor.wait_ge(d_cst, 16 * 3)
                        tensor.wait_ge(c_nsB, 1)
                        tensor.matmul(cps[:, 0:E], ones1[:], gbr_sb[:],
                                      start=True, stop=True).then_inc(c_nsA, 1)
                        tensor.wait_ge(c_nsB, 2)
                        tensor.wait_ge(d_cst3, 16 * 2)
                        tensor.matmul(cps[:, 0:E], ones1[:], ecapr_sb[:],
                                      start=True, stop=True).then_inc(c_nsA, 1)
                        tensor.wait_ge(d_cst2, 16 * 5)
                        for kt in range(NK):
                            tensor.wait_ge(s_xnu, kt + 1)
                            if kt >= 2:
                                tensor.wait_ge(s_ut, kt - 1)
                            ins = None
                            for hc in range(HC):
                                ins = tensor.transpose(
                                    tps[kt % 2][:, hc * 128:(hc + 1) * 128],
                                    u_bf[kt % 4][:, hc * 128:(hc + 1) * 128],
                                    eyeb_sb[:],
                                )
                            ins.then_inc(s_tp, 1)
                        tensor.drain().then_inc(bar, 1)

                    @blockA.gpsimd
                    def _(gpsimd):
                        nown = 0
                        for kt in range(NK):
                            p = kt % 4
                            gpsimd.wait_ge(s_xnu, kt + 1)
                            gpsimd.dma_start(out=scr_v[kt], in_=xn_sb[p][:]).then_inc(s_vw, 16)
                            if kt < NQT:
                                nown += 1
                                gpsimd.dma_start(out=scr_xn[kt], in_=xn_sb[p][:]).then_inc(s_xnw, 16)
                            gpsimd.wait_ge(s_vw, 16 * (kt + 1))
                            gpsimd.wait_ge(s_xnw, 16 * nown)
                            gpsimd.nop().then_inc(s_vfree, 1)
                        gpsimd.drain().then_inc(bar, 1)

            with ExitStack() as bctx:
                def bsb(name, shape, dtype):
                    return bctx.enter_context(nc.sbuf_tensor(name, shape, dtype))

                def bps(name, shape, dtype=F32):
                    return bctx.enter_context(nc.psum_tensor(name, shape, dtype))

                vbuf = [bsb(f"vb{i}", [128, VPL * H], BF16) for i in range(NVB)]
                pt_sb = [bsb(f"pt{i}", [128, QBS], BF16) for i in range(2)]
                xq_sb = bsb("xq", [128, QTPB * H], F32)
                xnq_sb = bsb("xnq", [128, QTPB * H], F32)
                pbuf = bsb("pbuf", [128, QTPB * H], F32)
                invz = bsb("invz", [128, QTPB], F32)
                dots = [bps(f"dots{i}", [128, QBS]) for i in range(2)]
                o_ps = [[bps(f"o{j}_{c}", [128, 512]) for c in range(2)]
                        for j in range(QTPB)]
                z_ps = [bps(f"z{j}", [128, 1]) for j in range(QTPB)]

                with nc.Block("phB") as blockB:

                    @blockB.tensor
                    def _(tensor):
                        def emit_pv(qb, kt):
                            m = qb * NK + kt
                            if kt == 0 and qb >= 1:
                                tensor.wait_ge(s_evq, qb)
                            tensor.wait_ge(s_exp, m + 1)
                            gld = qb * NVL + kt // VPL
                            tensor.wait_ge(s_vld[gld % NVB], 16 * (gld // NVB + 1))
                            vb = vbuf[gld % NVB]
                            voff = (kt % VPL) * H
                            ins = None
                            for j in range(QTPB):
                                lhsT = pt_sb[m % 2][:, j * 128:(j + 1) * 128]
                                tensor.matmul(o_ps[j][0][:], lhsT,
                                              vb[:, voff: voff + 512],
                                              start=(kt == 0), stop=(kt == NK - 1))
                                tensor.matmul(o_ps[j][1][:], lhsT,
                                              vb[:, voff + 512: voff + 1024],
                                              start=(kt == 0), stop=(kt == NK - 1))
                                ins = tensor.matmul(z_ps[j][:], lhsT, onesc_bf[:],
                                                    start=(kt == 0), stop=(kt == NK - 1))
                            ins.then_inc(s_pv, 1)

                        LKA = 1
                        tensor.wait_ge(bar, 5)
                        for qb in range(NQB):
                            for kt in range(NK):
                                n = qb * NK + kt
                                if n >= 2:
                                    tensor.wait_ge(s_exp, n - 1)
                                ins = None
                                for hc in range(HC):
                                    ins = tensor.matmul(
                                        dots[n % 2][:],
                                        uT[:, hc * S + kt * 128: hc * S + kt * 128 + 128],
                                        uT[:, hc * S + qb * QBS: hc * S + qb * QBS + QBS],
                                        start=(hc == 0), stop=(hc == HC - 1),
                                    )
                                ins.then_inc(s_dots, 1)
                                if kt >= LKA:
                                    emit_pv(qb, kt - LKA)
                            for kt in range(NK - LKA, NK):
                                emit_pv(qb, kt)
                        tensor.drain().then_inc(bar, 1)

                    @blockB.scalar
                    def _(scalar):
                        scalar.wait_ge(bar, 5)
                        for n in range(NQB * NK):
                            scalar.wait_ge(s_dots, n + 1)
                            if n >= 2:
                                scalar.wait_ge(s_pv, n - 1)
                            scalar.activation(out=pt_sb[n % 2][:], in_=dots[n % 2][:],
                                              func=AF.Exp).then_inc(s_exp, 1)
                        scalar.drain().then_inc(bar, 1)

                    @blockB.vector
                    def _(vector):
                        vector.wait_ge(bar, 5)
                        for qb in range(NQB):
                            vector.wait_ge(s_pv, NK * (qb + 1))
                            vector.wait_ge(s_xq, 64 * (qb + 1))
                            if qb >= 1:
                                vector.wait_ge(s_pst, 32 * qb)
                            for j in range(QTPB):
                                vector.reciprocal(invz[:, j:j + 1], z_ps[j][:])
                                vector.drain()
                                for c in range(2):
                                    vector.scalar_tensor_tensor(
                                        out=pbuf[:, j * H + c * 512: j * H + c * 512 + 512],
                                        in0=o_ps[j][c][:], scalar=invz[:, j:j + 1],
                                        in1=xq_sb[:, j * H + c * 512: j * H + c * 512 + 512],
                                        op0=ALU.mult, op1=ALU.add)
                                vector.drain()
                                vector.tensor_add(pbuf[:, j * H:(j + 1) * H],
                                                  pbuf[:, j * H:(j + 1) * H],
                                                  xnq_sb[:, j * H:(j + 1) * H])
                                vector.drain()
                            vector.drain().then_inc(s_evq, 1)
                        vector.drain().then_inc(bar, 1)

                    @blockB.sync
                    def _(sync):
                        sync.wait_ge(bar, 5)
                        for qb in range(NQB):
                            for vj in range(NVL):
                                g = qb * NVL + vj
                                if g >= NVB:
                                    sync.wait_ge(s_pv, VPL * (g - NVB) + VPL)
                                src = scr_v[vj * VPL: (vj + 1) * VPL].rearrange(
                                    "t p h -> p t h")
                                dst = vbuf[g % NVB][:].rearrange(
                                    "p (t h) -> p t h", t=VPL)
                                sync.dma_start(out=dst,
                                               in_=src).then_inc(s_vld[g % NVB], 16)
                            sync.wait_ge(s_evq, qb)
                            for j in range(QTPB):
                                qt = qb * QTPB + j
                                sync.dma_start(out=xq_sb[:, j * H:(j + 1) * H],
                                               in_=xbt[qt]).then_inc(s_xq, 16)
                                sync.dma_start(out=xnq_sb[:, j * H:(j + 1) * H],
                                               in_=scr_xn[qt]).then_inc(s_xq, 16)
                            if qb >= 1:
                                for j in range(QTPB):
                                    sync.dma_start(
                                        out=scr_pt[(qb - 1) * QTPB + j],
                                        in_=pbuf[:, j * H:(j + 1) * H]).then_inc(s_pst, 16)
                        sync.wait_ge(s_evq, NQB)
                        for j in range(QTPB):
                            sync.dma_start(out=scr_pt[(NQB - 1) * QTPB + j],
                                           in_=pbuf[:, j * H:(j + 1) * H]).then_inc(s_pst, 16)
                        sync.wait_ge(s_pst, 32 * NQB)
                        sync.nop().then_inc(bar, 1)

                    @blockB.gpsimd
                    def _(gpsimd):
                        gpsimd.wait_ge(bar, 5)
                        for e in range(E):
                            s = s_wc0 if e < 2 else s_wcast
                            if e >= 2:
                                gpsimd.wait_ge(s_evq,
                                               min(e * NQB // E + 1, NQB - 1))
                            gpsimd.dma_start(out=scr_w1g[e],
                                             in_=w1gluT_d[e]).then_inc(s, 16)
                            gpsimd.dma_start(out=scr_w1l[e],
                                             in_=w1linT_d[e]).then_inc(s, 16)
                        gpsimd.wait_ge(s_wc0, 64)
                        gpsimd.wait_ge(s_wcast, 16 * 2 * (E - 2))
                        gpsimd.nop().then_inc(bar, 1)

        CAP = cap
        NSLT = CAP // 128
        SLOTS = E * CAP
        SC = min(512, CAP)
        NSC = CAP // SC
        NM1 = FC * NSC
        NT = NQT
        BIGF = 1.0e7
        with ExitStack() as cctx:
            def csb(name, shape, dtype):
                return cctx.enter_context(nc.sbuf_tensor(name, shape, dtype))

            def cps_(name, shape, dtype=F32):
                return cctx.enter_context(nc.psum_tensor(name, shape, dtype))

            tT_f32 = csb("tT_f32", [128, H], F32)
            tgT = csb("tgT", [128, HC * CAP], BF16)
            hwT = csb("hwT", [128, FC * CAP], BF16)
            wglu = csb("wglu", [128, HC * F], BF16)
            wlin = csb("wlin", [128, HC * F], BF16)
            w2b = csb("w2b", [128, FC * H], BF16)
            xnc = [csb(f"xnc{i}", [128, H], F32) for i in range(2)]
            t_nat = csb("tnat", [128, H], F32)
            cst = [csb(f"cstC{i}", [128, 4], F32) for i in range(2)]
            g_all = csb("g_all", [128, NT * E], F32)
            m8s = csb("m8s", [128, 8], F32)
            scr8 = csb("scr8", [128, 8], F32)
            mskd = csb("mskd", [128, E], F32)
            msk_all = csb("msk_all", [128, NT * E], F32)
            nm1 = csb("nm1", [128, NT], F32)
            es_all = csb("es_all", [128, NT * E], F32)
            c_all = csb("c_all", [128, NT * E], F32)
            zc = csb("zc", [128, 2], F32)
            cT_sb = csb("cT_sb", [E, 128], F32)
            asbuf = [csb(f"asw{i}", [128, SC], F32) for i in range(2)]
            bsbuf = [csb(f"bsw{i}", [128, SC], F32) for i in range(2)]
            ssbuf = [csb(f"ssw{i}", [128, SC], F32) for i in range(2)]
            ptc = [csb(f"ptc{i}", [128, H], F32) for i in range(2)]
            tg_sb = [csb(f"tg{i}", [128, H], BF16) for i in range(2)]
            obuf = [csb(f"obuf{i}", [128, H], F32) for i in range(2)]
            gb4 = [csb(f"gb4_{i}", [128, H], F32) for i in range(2)]
            soff_sb = csb("soff_sb", [128, E], F32)
            kidx_sb = csb("kidx_sb", [128, E], F32)
            kscr = csb("kscr", [128, E], F32)
            kscr2 = csb("kscr2", [128, E], F32)
            kscr_i = csb("kscr_i", [128, E], I32)
            kscr2_i = csb("kscr2_i", [128, E], I32)
            sslot_sb = csb("sslot_sb", [128, E], F32)
            soff4 = csb("soff4", [128, NT * 4], F32)
            s4 = csb("s4", [128, NT * 4], F32)
            slotmap_i = csb("slotmap_i", [128, NT * 4], I32)
            sc4_i = csb("sc4_i", [128, NT * 4], I32)
            ck_all = csb("ck_all", [128, NT * 4], F32)
            iota_sb = csb("iota_sb", [128, NT], I32)
            totmat = csb("totmat", [128, E], F32)
            totrow = csb("totrow", [1, NT * E], F32)
            trow0 = totrow
            ksc8 = csb("ksc8", [128, 8 * E], F32)
            cstall = csb("cstall", [128, NT], F32)
            cst3a = csb("cst3a", [128, NT], F32)
            zall = csb("zall", [128, NT], F32)
            zinv = csb("zinv", [128, NT], F32)
            idx_sb = csb("idx_sb", [128, SLOTS // 128], I32)

            glu_ps = [cps_(f"glu{i}", [128, SC]) for i in range(2)]
            lin_ps = [cps_(f"lin{i}", [128, SC]) for i in range(2)]
            oa_ps = cps_("oa", [128, 512])
            ob_ps = cps_("ob", [128, 512])
            tps_bf = [cctx.enter_context(nc.psum_tensor(f"tpsC{i}", [128, 1024], BF16))
                      for i in range(2)]
            gb4 = gb4 + [t_nat, tT_f32]
            ranks_sb = es_all
            xn4 = xnc

            scr_tb_f = scr_tb.rearrange("t p h -> (t p) h")
            scr_os_r = scr_os.rearrange("(t p) h -> t p h", p=128)

            with nc.Block("phC") as blockC:

                @blockC.sync
                def _(sync):
                    sync.wait_ge(bar, 10)
                    sync.dma_start(out=iota_sb[:], in_=iota_t).then_inc(s_iot, 16)

                    def w1_load(e):
                        cpg = HC // WGR
                        for wsb, wdr in ((wglu, scr_w1g), (wlin, scr_w1l)):
                            src_r = wdr[e].rearrange("(c p) f -> p c f", p=128)
                            dst_r = wsb[:].rearrange("p (c f) -> p c f", c=HC)
                            for gr in range(WGR):
                                sync.dma_start(
                                    out=dst_r[:, gr * cpg:(gr + 1) * cpg, :],
                                    in_=src_r[:, gr * cpg:(gr + 1) * cpg, :],
                                ).then_inc(s_wda[0], 16)

                    sync.wait_ge(s_wc0, 64)
                    w1_load(0)
                    for qt in range(NT):
                        if qt >= 2:
                            sync.wait_ge(cA, qt - 1)
                        sync.dma_start(out=xnc[qt % 2][:],
                                       in_=scr_xn[qt]).then_inc(c_gx[qt % 2], 16)
                    for qt in range(NT):
                        if qt < 2:
                            sync.wait_ge(cA, NT - 1 + qt)
                        else:
                            sync.wait_ge(cD, qt - 1)
                        sync.dma_start(out=xnc[qt % 2][:],
                                       in_=scr_xn[qt]).then_inc(c_gx[qt % 2], 16)
                    def os_writes(e, lo, hi):
                        for sl in range(lo, hi):
                            tp = e * NSLT + sl
                            sync.wait_ge(s_cpA, tp + 1)
                            sync.dma_start(
                                out=scr_os_r[tp],
                                in_=obuf[sl % 2][:]).then_inc(s_osw[sl % 2], 16)

                    sync.wait_ge(s_m1, NM1)
                    w1_load(1)
                    os_writes(0, 0, NSLT)
                    sync.wait_ge(s_wcast, 16 * 2 * (E - 2))
                    for e in range(2, E):
                        sync.wait_ge(s_m1, NM1 * e)
                        w1_load(e)
                        os_writes(e - 1, 0, NSLT)
                    os_writes(E - 1, 0, NSLT)
                    for qt in range(NT):
                        if qt < 2:
                            sync.wait_ge(cD, NT)
                        else:
                            sync.wait_ge(s_fs, qt - 1)
                            sync.wait_ge(s_out[qt % 2], 16 * (qt // 2))
                        sync.dma_start(out=ptc[qt % 2][:],
                                       in_=scr_pt[qt]).then_inc(s_fl[qt % 2], 16)
                        sync.dma_start(out=xn4[qt % 2][:],
                                       in_=scr_xn[qt]).then_inc(s_fl[qt % 2], 16)
                    sync.wait_ge(s_out[0], 16 * ((NT + 1) // 2))
                    sync.wait_ge(s_out[1], 16 * (NT // 2))

                @blockC.gpsimd
                def _(gpsimd):
                    reg_slots = gpsimd.register("bc_slots").__enter__()
                    reg_tq = gpsimd.register("bc_tq").__enter__()
                    gpsimd.reg_mov(reg_slots, SLOTS - 1)
                    gpsimd.reg_mov(reg_tq, Tq - 1)
                    gpsimd.wait_ge(bar, 10)
                    for qt in range(NT):
                        gpsimd.wait_ge(cD, qt + 1)
                        gpsimd.dma_start(out=scr_tb[qt], in_=t_nat[:]).then_inc(s_tbw, 16)
                    gpsimd.wait_ge(c_zs, 1)
                    gpsimd.dma_start(out=scr_idx.rearrange("(p c) -> p c", p=128),
                                     in_=idx_sb[:]).then_inc(s_zi, 16)
                    gpsimd.wait_ge(rkB, NT)
                    gpsimd.dma_start(out=scr_bas.rearrange("(o c) -> o c", o=1),
                                     in_=totrow[:]).then_inc(s_trw, 16)
                    gpsimd.wait_ge(s_trw, 16)
                    gpsimd.dma_start(out=totmat[0:NT, 0:E],
                                     in_=scr_bas.rearrange("(p e) -> p e", p=NT)
                                     ).then_inc(s_trw, 16)
                    gpsimd.wait_ge(rkC, 2)
                    gpsimd.dma_start(out=scr_bas.rearrange("(p e) -> p e", p=NT),
                                     in_=totmat[0:NT, 0:E]).then_inc(s_trw, 16)
                    gpsimd.wait_ge(s_trw, 48)
                    gpsimd.dma_start(out=trow0[:],
                                     in_=scr_bas.rearrange("(o c) -> o c", o=1)
                                     ).then_inc(s_trw, 16)
                    for qt in range(NT):
                        gpsimd.wait_ge(s_dsp, qt + 1)
                        if qt == 0:
                            gpsimd.wait_ge(s_zi, 16)
                            gpsimd.wait_ge(s_iot, 16)
                        for k in range(4):
                            c = qt * 4 + k
                            gpsimd.indirect_dma_start(
                                out=scr_idx.rearrange("(s q) -> s q", q=1),
                                out_offset=bass.IndirectOffsetOnAxis(
                                    sc4_i[:, c:c + 1], 0),
                                in_=iota_sb[:, qt:qt + 1],
                                in_offset=None,
                                bounds_check=reg_slots,
                                oob_is_err=False,
                            ).then_inc(s_sct, 16)
                    gpsimd.wait_ge(s_sct, 16 * 4 * NT)
                    gpsimd.dma_start(out=idx_sb[:],
                                     in_=scr_idx.rearrange("(p c) -> p c", p=128)
                                     ).then_inc(s_ild, 16)
                    for e in range(E):
                        if e == 0:
                            gpsimd.wait_ge(s_ild, 16)
                            gpsimd.wait_ge(s_tbw, 16 * NT)
                        for sl in range(NSLT):
                            te = e * NSLT + sl
                            if te >= 2:
                                gpsimd.wait_ge(cTg, te - 1)
                            gpsimd.indirect_dma_start(
                                out=tg_sb[sl % 2][:],
                                out_offset=None,
                                in_=scr_tb_f,
                                in_offset=bass.IndirectOffsetOnAxis(
                                    idx_sb[:, te:te + 1], 0),
                                bounds_check=reg_tq,
                                oob_is_err=False,
                            ).then_inc(s_tgg[sl % 2], 16)
                        if e >= 1:
                            gpsimd.wait_ge(s_m2, NSLT * e)
                        src_r = w2T_d[e].rearrange("(c p) f -> p c f", p=128)
                        dst_r = w2b[:].rearrange("p (c f) -> p c f", c=FC)
                        for gr in range(WGR):
                            gpsimd.dma_start(
                                out=dst_r[:, gr * (FC // WGR):(gr + 1) * (FC // WGR), :],
                                in_=src_r[:, gr * (FC // WGR):(gr + 1) * (FC // WGR), :],
                            ).then_inc(s_wdb[0], 16)
                    gpsimd.wait_ge(s_osw[0], 16 * ((NSLT * E + 1) // 2))
                    gpsimd.wait_ge(s_osw[1], 16 * ((NSLT * E) // 2))
                    for qt in range(NT):
                        for k in range(4):
                            gk = qt * 4 + k
                            if gk >= 4:
                                gpsimd.wait_ge(s_g4e, gk - 3)
                            gpsimd.indirect_dma_start(
                                out=gb4[gk % 4][:],
                                out_offset=None,
                                in_=scr_os,
                                in_offset=bass.IndirectOffsetOnAxis(
                                    slotmap_i[:, gk:gk + 1], 0),
                                bounds_check=reg_slots,
                                oob_is_err=False,
                            ).then_inc(s_g4[gk % 4], 16)
                    gpsimd.drain()

                @blockC.scalar
                def _(scalar):
                    scalar.wait_ge(bar, 10)
                    for qt in range(NT):
                        scalar.wait_ge(c_gx[qt % 2], 16 * (qt // 2 + 1))
                        scalar.activation(out=obuf[qt % 2][:], in_=xnc[qt % 2][:],
                                          func=AF.Square,
                                          accum_out=cstall[:, qt:qt + 1])
                        scalar.drain().then_inc(cA, 1)
                    scalar.wait_ge(cB, 1)
                    scalar.activation(out=cst3a[:], in_=cst3a[:], func=AF.Sqrt)
                    scalar.drain().then_inc(cC, 1)
                    for qt in range(NT):
                        scalar.wait_ge(cTp, qt + 1)
                        ins = None
                        for hc in range(HC):
                            pssl = (oa_ps if hc < 4 else ob_ps)[
                                :, (hc % 4) * 128:((hc % 4) + 1) * 128]
                            ins = scalar.activation(
                                out=tT_f32[:, hc * 128:(hc + 1) * 128],
                                in_=pssl, func=AF.Copy,
                                scale=ns_col[:, hc:hc + 1])
                        ins.then_inc(cEv, 1)
                    for qt in range(NT):
                        scalar.wait_ge(cMk, qt + 1)
                        scalar.activation(out=es_all[:, qt * E:(qt + 1) * E],
                                          in_=g_all[:, qt * E:(qt + 1) * E],
                                          func=AF.Exp,
                                          bias=nm1[:, qt:qt + 1]).then_inc(c1e, 1)
                    for e in range(E):
                        for fc in range(FC):
                            for sc in range(NSC):
                                gi = e * NM1 + fc * NSC + sc
                                par = gi % 2
                                scalar.wait_ge(s_m1, gi + 1)
                                if gi >= 2:
                                    scalar.wait_ge(s_sw, gi - 1)
                                bcol = e * FC + fc
                                scalar.activation(out=asbuf[par][:],
                                                  in_=glu_ps[par][:], func=AF.Relu,
                                                  scale=-1.0,
                                                  bias=b1g_sb[:, bcol:bcol + 1])
                                scalar.activation(out=bsbuf[par][:],
                                                  in_=lin_ps[par][:], func=AF.Relu,
                                                  scale=-1.0,
                                                  bias=b1l_sb[:, bcol:bcol + 1])
                                scalar.drain()
                                scalar.activation(out=ssbuf[par][:],
                                                  in_=asbuf[par][:], func=AF.Silu,
                                                  scale=-ALPHA,
                                                  bias=cbias[:, 0:1])
                                scalar.activation(out=bsbuf[par][:],
                                                  in_=bsbuf[par][:], func=AF.Relu,
                                                  scale=-1.0,
                                                  bias=cbias[:, 1:2]).then_inc(s_si, 1)
                        for sl in range(NSLT):
                            te = e * NSLT + sl
                            scalar.wait_ge(s_m2, te + 1)
                            if te >= 2:
                                scalar.wait_ge(s_osw[sl % 2], 16 * (te // 2))
                            pa = oa_ps if sl % 2 == 0 else glu_ps[0]
                            pb = ob_ps if sl % 2 == 0 else lin_ps[0]
                            scalar.activation(out=obuf[sl % 2][:, 0:512], in_=pa[:],
                                              func=AF.Copy)
                            scalar.activation(out=obuf[sl % 2][:, 512:1024], in_=pb[:],
                                              func=AF.Copy).then_inc(s_cpA, 1)
                    for qt in range(NT):
                        scalar.wait_ge(s_fs, qt + 1)
                        scalar.dma_start(out=out_t[qt],
                                         in_=ptc[qt % 2][:]).then_inc(s_out[qt % 2], 16)
                    scalar.wait_ge(s_out[0], 16 * ((NT + 1) // 2))
                    scalar.wait_ge(s_out[1], 16 * (NT // 2))

                @blockC.tensor
                def _(tensor):
                    tensor.wait_ge(bar, 10)
                    for qt in range(NT):
                        tensor.wait_ge(cD, qt + 1)
                        if qt >= 1:
                            tensor.wait_ge(cEv, qt)
                        ins = None
                        for hc in range(HC):
                            dst = oa_ps if hc < 4 else ob_ps
                            ins = tensor.transpose(
                                dst[:, (hc % 4) * 128:((hc % 4) + 1) * 128],
                                t_nat[:, hc * 128:(hc + 1) * 128],
                                eyef_sb[:],
                            )
                        ins.then_inc(cTp, 1)
                        tensor.wait_ge(cEv, qt + 1)
                        if qt >= 1:
                            tensor.wait_ge(cMk, qt)
                        ins = None
                        for hc in range(HC):
                            ins = tensor.matmul(
                                glu_ps[0][:, 0:E],
                                tT_f32[:, hc * 128:(hc + 1) * 128],
                                gwT[:, hc * E:(hc + 1) * E],
                                start=(hc == 0), stop=(hc == HC - 1),
                            )
                        ins.then_inc(cGm, 1)
                    for qt in range(NT):
                        tensor.wait_ge(cMk, qt + 1)
                        if qt >= 1:
                            tensor.wait_ge(rkB, qt)
                        tensor.matmul(lin_ps[1][:, 0:E], lst_sb[:],
                                      msk_all[:, qt * E:(qt + 1) * E],
                                      start=True, stop=True).then_inc(rkA, 1)
                        tensor.matmul(oa_ps[0:1, 0:E], onesf_col[:],
                                      msk_all[:, qt * E:(qt + 1) * E],
                                      start=True, stop=True).then_inc(rkA, 1)
                    tensor.wait_ge(s_trw, 32)
                    tensor.matmul(lin_ps[1][:, 0:E], lst_sb[0:NT, :],
                                  totmat[0:NT, 0:E],
                                  start=True, stop=True).then_inc(rkC, 1)
                    tensor.wait_ge(s_trw, 64)
                    for qt in range(NT):
                        if qt >= 1:
                            tensor.wait_ge(s_dsp, qt)
                        tensor.matmul(ob_ps[:, 0:E], ones1[:],
                                      trow0[:, qt * E:(qt + 1) * E],
                                      start=True, stop=True).then_inc(rkD, 1)
                    def tg_transpose(te):
                        sl = te % NSLT
                        tensor.wait_ge(s_tgg[sl % 2], 16 * (te // 2 + 1))
                        if te >= 2:
                            tensor.wait_ge(s_tge, te - 1)
                        ins = None
                        for hc in range(HC):
                            ins = tensor.transpose(
                                tps_bf[te % 2][:, hc * 128:(hc + 1) * 128],
                                tg_sb[sl % 2][:, hc * 128:(hc + 1) * 128],
                                eyeb_sb[:],
                            )
                        ins.then_inc(cTg, 1)

                    for e in range(E):
                        if e == 0:
                            for sl in range(NSLT):
                                tg_transpose(sl)
                        tensor.wait_ge(s_tge, NSLT * (e + 1))
                        tensor.wait_ge(s_wda[0], 16 * 8 * (e + 1))
                        if e >= 1:
                            tensor.wait_ge(s_cpA, NSLT * e)
                        for fc in range(FC):
                            for sc in range(NSC):
                                gi = e * NM1 + fc * NSC + sc
                                par = gi % 2
                                if gi >= 2:
                                    tensor.wait_ge(s_sw, gi - 1)
                                for hc in range(HC):
                                    tensor.matmul(
                                        glu_ps[par][:],
                                        wglu[:, hc * F + fc * 128:
                                             hc * F + fc * 128 + 128],
                                        tgT[:, hc * CAP + sc * SC:
                                            hc * CAP + sc * SC + SC],
                                        start=(hc == 0), stop=(hc == HC - 1),
                                    )
                                ins = None
                                for hc in range(HC):
                                    ins = tensor.matmul(
                                        lin_ps[par][:],
                                        wlin[:, hc * F + fc * 128:
                                             hc * F + fc * 128 + 128],
                                        tgT[:, hc * CAP + sc * SC:
                                            hc * CAP + sc * SC + SC],
                                        start=(hc == 0), stop=(hc == HC - 1),
                                    )
                                ins.then_inc(s_m1, 1)
                        tensor.wait_ge(s_wdb[0], 16 * 4 * (e + 1))
                        for sl in range(NSLT):
                            te = e * NSLT + sl
                            if te >= 2:
                                tensor.wait_ge(s_cpA, te - 1)
                            if sl == 1:
                                tensor.wait_ge(s_si, NM1 * (e + 1))
                            pa = oa_ps if sl % 2 == 0 else glu_ps[0]
                            pb = ob_ps if sl % 2 == 0 else lin_ps[0]
                            ins = None
                            for fc in range(FC):
                                tensor.wait_ge(s_sw, e * NM1 + fc * NSC
                                               + (sl * 128) // SC + 1)
                                lh = hwT[:, fc * CAP + sl * 128: fc * CAP + sl * 128 + 128]
                                tensor.matmul(pa[:], lh,
                                              w2b[:, fc * H: fc * H + 512],
                                              start=(fc == 0), stop=(fc == FC - 1))
                                ins = tensor.matmul(pb[:], lh,
                                                    w2b[:, fc * H + 512: fc * H + 1024],
                                                    start=(fc == 0), stop=(fc == FC - 1))
                            ins.then_inc(s_m2, 1)
                            if e + 1 < E:
                                tg_transpose((e + 1) * NSLT + sl)
                    tensor.wait_ge(s_cpA, NSLT * E)
                    for qt in range(NT):
                        if qt >= 1:
                            tensor.wait_ge(c2c, qt)
                            tensor.wait_ge(c2e, qt)
                        tensor.transpose(
                            lin_ps[1][0:E, 0:128],
                            c_all[:, qt * E:(qt + 1) * E],
                            eyef_sb[:],
                        ).then_inc(c2t, 1)
                        tensor.wait_ge(c2c, qt + 1)
                        tensor.matmul(oa_ps[:], cT_sb[:], b2s[:, 0:512],
                                      start=True, stop=True)
                        tensor.matmul(ob_ps[:], cT_sb[:], b2s[:, 512:1024],
                                      start=True, stop=True).then_inc(c2m, 1)
                    tensor.drain()

                @blockC.vector
                def _(vector):
                    vector.wait_ge(bar, 10)
                    vector.memset(idx_sb[:], 0)
                    vector.drain().then_inc(c_zs, 1)
                    vector.wait_ge(cA, NT)
                    vector.tensor_scalar(out=cstall[:], in0=cstall[:],
                                         scalar1=1.0 / H, scalar2=RMS_EPS,
                                         op0=ALU.mult, op1=ALU.add)
                    vector.drain()
                    vector.reciprocal(cst3a[:], cstall[:])
                    vector.drain().then_inc(cB, 1)
                    vector.wait_ge(cC, 1)
                    for qt in range(NT):
                        p = qt % 2
                        vector.wait_ge(c_gx[qt % 2], 16 * (NT // 2 + qt // 2 + 1))
                        if qt >= 1:
                            vector.wait_ge(cTp, qt)
                            vector.wait_ge(s_tbw, 16 * qt)
                        vector.tensor_scalar_mul(t_nat[:], xnc[p][:],
                                                 cst3a[:, qt:qt + 1])
                        vector.drain().then_inc(cD, 1)
                        vector.wait_ge(cGm, qt + 1)
                        vector.scalar_tensor_tensor(
                            out=g_all[:, qt * E:(qt + 1) * E],
                            in0=glu_ps[0][:, 0:E], scalar=1.0, in1=gb_bcast[:],
                            op0=ALU.mult, op1=ALU.add)
                        vector.drain()
                        gsl = g_all[:, qt * E:(qt + 1) * E]
                        vector.max(out=m8s[:], in_=gsl)
                        vector.drain()
                        vector.tensor_copy(out=scr8[:, 0:4], in_=m8s[:, 0:4])
                        vector.memset(scr8[:, 4:8], NEG_FILL)
                        vector.drain()
                        vector.match_replace(out=mskd[:], in_to_replace=scr8[:],
                                             in_values=gsl, imm_value=NEG_BIG)
                        vector.drain()
                        vector.tensor_sub(msk_all[:, qt * E:(qt + 1) * E], gsl, mskd[:])
                        vector.drain()
                        vector.tensor_scalar_min(msk_all[:, qt * E:(qt + 1) * E],
                                                 msk_all[:, qt * E:(qt + 1) * E], 1.0)
                        vector.tensor_scalar_mul(nm1[:, qt:qt + 1], m8s[:, 0:1], -1.0)
                        vector.drain().then_inc(cMk, 1)
                    vector.wait_ge(c1e, NT)
                    vector.tensor_mul(es_all[:], es_all[:], msk_all[:])
                    vector.drain()
                    for qt in range(NT):
                        vector.reduce_sum(zall[:, qt:qt + 1],
                                          es_all[:, qt * E:(qt + 1) * E], axis=AX.X)
                    vector.drain()
                    vector.reciprocal(zinv[:], zall[:])
                    vector.drain()
                    for qt in range(NT):
                        vector.tensor_scalar(out=c_all[:, qt * E:(qt + 1) * E],
                                             in0=es_all[:, qt * E:(qt + 1) * E],
                                             scalar1=zinv[:, qt:qt + 1],
                                             scalar2=1.0 / ALPHA,
                                             op0=ALU.mult, op1=ALU.mult)
                    vector.drain().then_inc(c1c, NT)
                    for qt in range(NT):
                        vector.wait_ge(rkA, 2 * (qt + 1))
                        vector.tensor_copy(out=ranks_sb[:, qt * E:(qt + 1) * E],
                                           in_=lin_ps[1][:, 0:E])
                        vector.tensor_copy(out=totrow[:, qt * E:(qt + 1) * E],
                                           in_=oa_ps[0:1, 0:E])
                        vector.drain().then_inc(rkB, 1)
                    vector.wait_ge(rkC, 1)
                    vector.tensor_copy(out=totmat[0:NT, 0:E], in_=lin_ps[1][0:NT, 0:E])
                    vector.drain().then_inc(rkC, 1)
                    for qt in range(NT):
                        vector.wait_ge(rkD, qt + 1)
                        rsl = ranks_sb[:, qt * E:(qt + 1) * E]
                        vector.scalar_tensor_tensor(out=rsl, in0=ob_ps[:, 0:E],
                                                    scalar=1.0, in1=rsl,
                                                    op0=ALU.mult, op1=ALU.add)
                        vector.drain()
                        vector.tensor_add(sslot_sb[:], rsl, ecap_bc[:])
                        vector.tensor_scalar(out=kscr[:], in0=rsl,
                                             scalar1=float(CAP), scalar2=None,
                                             op0=ALU.is_ge)
                        vector.drain()
                        vector.scalar_tensor_tensor(out=sslot_sb[:], in0=kscr[:],
                                                    scalar=BIGF, in1=sslot_sb[:],
                                                    op0=ALU.mult, op1=ALU.add)
                        vector.tensor_copy(out=kscr_i[:], in_=rsl)
                        vector.drain()
                        vector.tensor_scalar(out=kscr2_i[:], in0=kscr_i[:],
                                             scalar1=127, scalar2=None,
                                             op0=ALU.bitwise_and)
                        vector.drain()
                        vector.tensor_scalar(out=kscr_i[:], in0=kscr_i[:],
                                             scalar1=7, scalar2=None,
                                             op0=ALU.arith_shift_right)
                        vector.drain()
                        vector.tensor_copy(out=kscr2[:], in_=kscr2_i[:])
                        vector.tensor_copy(out=soff_sb[:], in_=kscr_i[:])
                        vector.drain()
                        vector.scalar_tensor_tensor(out=soff_sb[:], in0=kscr2[:],
                                                    scalar=float(SLOTS // 128),
                                                    in1=soff_sb[:],
                                                    op0=ALU.mult, op1=ALU.add)
                        vector.drain()
                        vector.scalar_tensor_tensor(out=soff_sb[:], in0=ecap_bc[:],
                                                    scalar=1.0 / 128.0, in1=soff_sb[:],
                                                    op0=ALU.mult, op1=ALU.add)
                        vector.drain()
                        vector.scalar_tensor_tensor(out=soff_sb[:], in0=kscr[:],
                                                    scalar=BIGF, in1=soff_sb[:],
                                                    op0=ALU.mult, op1=ALU.add)
                        msl = msk_all[:, qt * E:(qt + 1) * E]
                        vector.tensor_tensor_scan(out=kidx_sb[:], data0=msl, data1=msl,
                                                  initial=0.0, op0=ALU.add,
                                                  op1=ALU.bypass)
                        vector.drain()
                        vector.scalar_tensor_tensor(out=kidx_sb[:], in0=msl,
                                                    scalar=BIGF - 1.0, in1=kidx_sb[:],
                                                    op0=ALU.mult, op1=ALU.add)
                        vector.drain()
                        vector.tensor_scalar_add(kidx_sb[:], kidx_sb[:], -BIGF)
                        vector.drain()
                        for k in range(4):
                            c = qt * 4 + k
                            par = k % 2
                            mb = ksc8[:, par * E:par * E + E]
                            p0 = ksc8[:, (2 + par * 3) * E:(3 + par * 3) * E]
                            p1 = ksc8[:, (3 + par * 3) * E:(4 + par * 3) * E]
                            p2 = ksc8[:, (4 + par * 3) * E:(5 + par * 3) * E]
                            vector.tensor_scalar(out=mb, in0=kidx_sb[:],
                                                 scalar1=float(k), scalar2=None,
                                                 op0=ALU.is_equal)
                            vector.drain()
                            vector.tensor_mul(p0, mb, soff_sb[:])
                            vector.tensor_mul(p1, mb, sslot_sb[:])
                            vector.tensor_mul(p2, mb,
                                              c_all[:, qt * E:(qt + 1) * E])
                            vector.drain()
                            vector.reduce_sum(soff4[:, c:c + 1], p0, axis=AX.X)
                            vector.reduce_sum(s4[:, c:c + 1], p1, axis=AX.X)
                            vector.reduce_sum(ck_all[:, c:c + 1], p2, axis=AX.X)
                            vector.drain()
                        vector.tensor_copy(out=slotmap_i[:, qt * 4:(qt + 1) * 4],
                                           in_=s4[:, qt * 4:(qt + 1) * 4])
                        vector.tensor_copy(out=sc4_i[:, qt * 4:(qt + 1) * 4],
                                           in_=soff4[:, qt * 4:(qt + 1) * 4])
                        vector.drain().then_inc(s_dsp, 1)
                    for e in range(E):
                        for sl in range(NSLT):
                            te = e * NSLT + sl
                            vector.wait_ge(cTg, te + 1)
                            for hc in range(HC):
                                vector.tensor_scalar_mul(
                                    tgT[:, hc * CAP + sl * 128: hc * CAP + sl * 128 + 128],
                                    tps_bf[te % 2][:, hc * 128:(hc + 1) * 128],
                                    ns_col[:, hc:hc + 1])
                            vector.drain().then_inc(s_tge, 1)
                        for fc in range(FC):
                            for sc in range(NSC):
                                gi = e * NM1 + fc * NSC + sc
                                par = gi % 2
                                vector.wait_ge(s_si, gi + 1)
                                vector.scalar_tensor_tensor(
                                    out=hwT[:, fc * CAP + sc * SC:
                                            fc * CAP + sc * SC + SC],
                                    in0=bsbuf[par][:], scalar=-(LIMIT - 1.0),
                                    in1=ssbuf[par][:],
                                    op0=ALU.add, op1=ALU.mult).then_inc(s_sw, 1)
                    for qt in range(NT):
                        p = qt % 2
                        vector.wait_ge(c2t, qt + 1)
                        if qt >= 1:
                            vector.wait_ge(c2m, qt)
                        vector.tensor_copy(out=cT_sb[:], in_=lin_ps[1][0:E, 0:128])
                        vector.drain().then_inc(c2c, 1)
                        vector.wait_ge(c2m, qt + 1)
                        vector.wait_ge(s_fl[qt % 2], 32 * (qt // 2 + 1))
                        vector.tensor_add(ptc[p][:, 0:512], oa_ps[:], ptc[p][:, 0:512])
                        vector.tensor_add(ptc[p][:, 512:1024], ob_ps[:],
                                          ptc[p][:, 512:1024])
                        vector.drain()
                        vector.tensor_add(ptc[p][:], xn4[p][:], ptc[p][:])
                        vector.drain().then_inc(c2e, 1)
                        for k in range(4):
                            gk = qt * 4 + k
                            vector.wait_ge(s_g4[gk % 4], 16 * (gk // 4 + 1))
                            vector.scalar_tensor_tensor(
                                out=ptc[p][:], in0=gb4[gk % 4][:],
                                scalar=ck_all[:, gk:gk + 1], in1=ptc[p][:],
                                op0=ALU.mult, op1=ALU.add)
                            vector.drain().then_inc(s_g4e, 1)
                        vector.drain().then_inc(s_fs, 1)
                    vector.drain()

    return nc



def host_prep(norm_scale, gate_w, gate_b, w1, b1, w2, b2):
    ns = np.ascontiguousarray(np.asarray(norm_scale, dtype=np.float32))
    gw = np.asarray(gate_w, dtype=np.float32)
    gb = np.asarray(gate_b, dtype=np.float32)
    w1 = np.asarray(w1, dtype=np.float32)
    b1 = np.asarray(b1, dtype=np.float32)
    w2 = np.asarray(w2, dtype=np.float32)
    b2 = np.asarray(b2, dtype=np.float32)
    E, H, F = w2.shape[0], w2.shape[1], w2.shape[2]
    FC = F // 128
    return {
        "w1gluT": np.ascontiguousarray(w1[:, 0::2, :].transpose(0, 2, 1)),
        "w1linT": np.ascontiguousarray(w1[:, 1::2, :].transpose(0, 2, 1)),
        "w2T": np.ascontiguousarray(w2.transpose(0, 2, 1)),
        "gate_wT": np.ascontiguousarray(gw.T),
        "gate_b_row": np.ascontiguousarray(gb[None, :]),
        "ns_cols": np.ascontiguousarray(ns.reshape(-1, 128).T),
        "b1glu_cols": np.ascontiguousarray(
            b1[:, 0::2].reshape(E, FC, 128).transpose(0, 2, 1)),
        "b1lin_cols": np.ascontiguousarray(
            b1[:, 1::2].reshape(E, FC, 128).transpose(0, 2, 1)),
        "b2_in": np.ascontiguousarray(b2),
        "eye_bf": np.eye(128, dtype=ml_dtypes.bfloat16),
        "eye_f32": np.eye(128, dtype=np.float32),
        "lstrict": np.triu(np.ones((128, 128), dtype=np.float32), 1),
    }


def kernel(L0_attn_x, L0_ffn_norm_scale, L0_ffn_gate_w, L0_ffn_gate_b,
           L0_ffn_w1, L0_ffn_b1, L0_ffn_w2, L0_ffn_b2):
    x = np.asarray(L0_attn_x, dtype=np.float32)
    B, S, H = x.shape
    n_cores = 8
    halves = n_cores // B
    Tq = S // halves
    wmaps = host_prep(L0_ffn_norm_scale, L0_ffn_gate_w, L0_ffn_gate_b,
                      L0_ffn_w1, L0_ffn_b1, L0_ffn_w2, L0_ffn_b2)
    E = wmaps["b2_in"].shape[0]
    F = wmaps["w2T"].shape[1]

    cap = ((16 * Tq // E + 511) // 512) * 512
    nc = build_program(S=S, Tq=Tq, H=H, E=E, F=F, cap=cap)
    wmaps["ecap_row"] = (np.arange(E, dtype=np.float32) * cap)[None, :]
    wmaps["iota_t"] = np.ascontiguousarray(np.arange(Tq, dtype=np.int32).reshape(Tq // 128, 128).T)
    in_maps = []
    for c in range(n_cores):
        b = c // halves
        h = c % halves
        xbc = np.ascontiguousarray(np.roll(x[b], -h * Tq, axis=0))
        m = {"xb": xbc}
        m.update(wmaps)
        in_maps.append(m)
    res = run_bass_kernel_spmd(nc, in_maps, core_ids=list(range(n_cores)))
    out = np.empty((B, S, H), dtype=np.float32)
    for c in range(n_cores):
        b = c // halves
        h = c % halves
        out[b, h * Tq:(h + 1) * Tq, :] = res.results[c]["out"]
    return out
